# version 14
# baseline (speedup 1.0000x reference)
import numpy as np
import concourse.bass as bass
import concourse.mybir as mybir
from concourse.bass_utils import run_bass_kernel_spmd

F32 = mybir.dt.float32
BF16 = mybir.dt.bfloat16
FP8 = mybir.dt.float8e4
AF = mybir.ActivationFunctionType
ALU = mybir.AluOpType
AX = mybir.AxisListType

B = 4
S = 4096
DM = 1024
MEM = 256
FF = 4096
TS = 32
NC = 8
SEG = S // NC
NT = SEG // TS
DEC = 0.99
NW = DM * FF
W2_SCALAR = 20


class _Sem:
    def __init__(self, name, kind, owner):
        self.name = name
        self.kind = kind
        self.owner = owner
        self.count = 0
        self.handle = None


class _Prog:
    ENGS = ("sync", "gpsimd", "tensor", "vector", "scalar")

    def __init__(self):
        self.items = {e: [] for e in self.ENGS}

    def emit(self, eng, fn, deps=(), inc=None, inc_n=1):
        deps = tuple(d for d in deps if d is not None)
        if eng == "gpsimd" and inc is not None and inc.kind == "eng" and inc.count:
            deps = deps + ((inc, inc.count),)
        post = None
        if inc is not None:
            inc.count += inc_n
            post = (inc, inc.count)
        self.items[eng].append((fn, deps, inc, inc_n))
        return post

    def run_engine(self, engname, engobj):
        last = {}
        for fn, deps, inc, inc_n in self.items[engname]:
            need = {}
            for sem, val in deps:
                if val > need.get(sem, -1):
                    need[sem] = val
            for sem, val in need.items():
                if last.get(sem, -1) >= val:
                    continue
                engobj.wait_ge(sem.handle, val)
                last[sem] = val
            if fn is None:
                continue
            instr = fn(engobj)
            if inc is not None:
                instr.then_inc(inc.handle, inc_n)


def build_program():
    nc = bass.Bass("TRN2", target_bir_lowering=False, debug=False)
    P = _Prog()

    def dram_in(name, shape):
        return nc.dram_tensor(name, shape, F32, kind="ExternalInput").ap()

    xc = dram_in("xc", [B, SEG, DM])
    wqT = dram_in("wqT", [DM, MEM])
    wkT = dram_in("wkT", [DM, MEM])
    wvT = dram_in("wvT", [DM, MEM])
    woT = dram_in("woT", [MEM, DM])
    w1s = dram_in("w1s", [128, FF])
    w2s = dram_in("w2s", [512, DM])
    b1r = dram_in("b1r", [128, FF // 128])
    b2r = dram_in("b2r", [1, DM])
    gatei = dram_in("gatei", [1, 1])
    m0T = dram_in("m0T", [MEM, MEM])
    identf = dram_in("identf", [128, 128])
    ones128_d = dram_in("ones128", [128, 128])
    dmaskf_d = dram_in("dmaskf", [128, 128])
    dq2_d = dram_in("dq2", [128, 1])
    dvv_d = dram_in("dvv", [128, 1])
    pw_d = dram_in("pw", [128, 9])
    onescol_d = dram_in("onescol", [128, 1])
    yc = nc.dram_tensor("yc", [B, SEG, DM], F32, kind="ExternalOutput").ap()
    gin2 = nc.dram_tensor("gin2", [1, 2], F32).ap()
    gout2 = nc.dram_tensor("gout2", [1, 2], F32).ap()
    ginM = nc.dram_tensor("ginM", [MEM, MEM], F32).ap()
    goutM = nc.dram_tensor("goutM", [NC * MEM, MEM], F32).ap()
    gin3 = nc.dram_tensor("gin3", [8, 128, 1024], FP8).ap()
    gout3 = nc.dram_tensor("gout3", [64, 128, 1024], FP8).ap()

    lc = _Sem("lc", "dma", "sync")
    lw0 = _Sem("lw0", "dma", "sync")
    lw1 = _Sem("lw1", "dma", "sync")
    lx0 = _Sem("lx0", "dma", "sync")
    lx1 = _Sem("lx1", "dma", "sync")
    lyo = _Sem("lyo", "dma", "gpsimd")
    lgm = _Sem("lgm", "dma", "gpsimd")
    lag = _Sem("lag", "dma", "scalar")
    lar = _Sem("lar", "dma", "sync")
    lrr0 = _Sem("lrr0", "dma", "gpsimd")
    lrr1 = _Sem("lrr1", "dma", "gpsimd")
    lsh0 = _Sem("lsh0", "dma", "gpsimd")
    lsh1 = _Sem("lsh1", "dma", "gpsimd")
    lwt = _Sem("lwt", "dma", "gpsimd")
    lwr1 = _Sem("lwr1", "dma", "scalar")
    lwr2 = _Sem("lwr2", "dma", "scalar")
    tn = _Sem("tn", "eng", "tensor")
    vc = _Sem("vc", "eng", "vector")
    sc = _Sem("sc", "eng", "scalar")
    gp = _Sem("gp", "eng", "gpsimd")
    lx = (lx0, lx1)
    lw = (lw0, lw1)
    lrr = (lrr0, lrr1)

    import contextlib
    with contextlib.ExitStack() as _st:
        def sb(name, shape, dt):
            return _st.enter_context(nc.sbuf_tensor(name, shape, dt))

        def ps(name, shape, dt):
            return _st.enter_context(nc.psum_tensor(name, shape, dt))

        w1t = sb("w1t", [128, 8 * FF], FP8)
        w2t = sb("w2t", [128, 32 * DM], FP8)
        cWq = sb("cWq", [128, 8 * MEM], BF16)
        cWk = sb("cWk", [128, 8 * MEM], BF16)
        cWv = sb("cWv", [128, 8 * MEM], BF16)
        cWo = sb("cWo", [128, 2 * DM], BF16)
        xnT = sb("xnT", [128, 8 * (NT * 128)], BF16)
        qdT = sb("qdT", [128, NT * MEM], BF16)
        rin = sb("rin", [128, NT * MEM], BF16)
        hbuf = sb("hbuf", [128, 32 * 256], BF16)
        xst0 = sb("xst0", [128, DM], F32)
        xst1 = sb("xst1", [128, DM], F32)
        wst0 = sb("wst0", [128, 1024], F32)
        wst1 = sb("wst1", [128, 1024], F32)
        sgn1 = sb("sgn1", [128, 1024], BF16)
        sgn2 = sb("sgn2", [128, 1024], BF16)
        xnt_t = sb("xnt_t", [128, 2 * DM], BF16)
        qd_s = sb("qd_s", [128, 2 * MEM], BF16)
        k_s = sb("k_s", [128, 4 * MEM], BF16)
        v_s = sb("v_s", [128, 4 * MEM], BF16)
        vd_s = sb("vd_s", [128, 4 * MEM], BF16)
        kT_s = sb("kT_s", [128, 2 * MEM], BF16)
        rT_s = sb("rT_s", [128, MEM], BF16)
        ptb = sb("ptb", [128, 2 * 128], BF16)
        rtmp = sb("rtmp", [128, MEM], BF16)
        MT = sb("MT", [128, 512], F32)
        MTbs = sb("MTbs", [128, 2 * 512], BF16)
        dMb = sb("dMb", [128, 2 * 512], BF16)
        MsT = sb("MsT", [128, 512], F32)
        MsTb = sb("MsTb", [128, 512], BF16)
        agt = sb("agt", [128, 512], F32)
        ybuf = sb("ybuf", [128, DM], F32)
        yh = sb("yh", [128, 512], F32)
        dmaskc = sb("dmaskc", [128, 128], F32)
        identb = sb("identb", [128, 128], BF16)
        ones128f = sb("ones128f", [128, 128], F32)
        b2bc = sb("b2bc", [128, DM], BF16)
        dq2c = sb("dq2c", [128, 1], F32)
        dvvc = sb("dvvc", [128, 1], F32)
        pwc = sb("pwc", [128, 9], F32)
        onescf = sb("onescf", [128, 1], F32)
        b1rc = sb("b1rc", [128, 32], F32)
        sgf = sb("sgf", [1, 1], F32)
        sgs = sb("sgs", [1, 1], F32)
        gtmp = sb("gtmp", [128, 1], F32)
        wtt0 = sb("wtt0", [128, 1024], FP8)
        yat0 = sb("yat0", [128, DM], F32)
        yat1 = sb("yat1", [128, DM], F32)
        ttmp = sb("ttmp", [128, 2], F32)
        gateb = sb("gateb", [128, 1], F32)
        uvec = sb("uvec", [128, 1], F32)
        asum = sb("asum", [128, 16], F32)
        part2 = sb("part2", [128, 2], F32)
        tot12 = sb("tot12", [1, 2], F32)
        tot12g = sb("tot12g", [1, 2], F32)
        thr12 = sb("thr12", [1, 2], F32)
        thrb = sb("thrb", [128, 2], F32)
        nthrb = sb("nthrb", [128, 2], F32)
        lnbn = sb("lnbn", [128, 24], F32)
        lnagg = sb("lnagg", [128, 4], F32)
        lnstd = sb("lnstd", [128, 2], F32)
        lnrs = sb("lnrs", [128, 2], F32)
        lnmb = sb("lnmb", [128, 2], F32)
        epsc = sb("epsc", [128, 1], F32)
        pT0 = ps("pT0", [128, 1024], BF16)
        pT1 = ps("pT1", [128, 1024], BF16)
        pQK = ps("pQK", [128, 512], F32)
        pVS = ps("pVS", [128, 512], F32)
        pR = ps("pR", [128, 512], F32)
        pM = ps("pM", [128, 512], F32)
        pH0 = ps("pH0", [128, 512], F32)
        pH1 = ps("pH1", [128, 512], F32)
        for s in (lc, lw0, lw1, lx0, lx1, lyo, lgm, lag, lar, lrr0, lrr1, lsh0, lsh1, lwt, lwr1, lwr2, tn, vc, sc, gp):
            s.handle = _st.enter_context(nc.semaphore("s_" + s.name))
        block = _st.enter_context(nc.Block())

        pT = (pT0, pT1)
        xst = (xst0, xst1)
        wst = (wst0, wst1)
        pH = (pH0, pH1)
        pO = (pQK, pVS)
        E = P.emit

        E("sync", lambda e: e.dma_start(out=wst0[:, 0:128], in_=identf[:, :]),
          inc=lc, inc_n=16)
        for (dst, src_) in ((sgf, gatei), (dmaskc, dmaskf_d), (dq2c, dq2_d)):
            E("sync", lambda e, d2=dst, s2=src_: e.dma_start(out=d2[:, :], in_=s2[:, :]),
              inc=lc, inc_n=16)
        h_early = (lc, lc.count)
        h_xld = [None] * NT
        for i in range(2):
            for bb in range(B):
                h_xld[i] = E("sync", lambda e, i2=i, b2=bb: e.dma_start(
                    out=xst[i2][b2 * TS:(b2 + 1) * TS, :],
                    in_=xc[b2, i2 * TS:(i2 + 1) * TS, :]), inc=lx[i], inc_n=16)
        h_idc = E("vector", lambda e: e.tensor_copy(identb[:, :], wst0[:, 0:128]),
                  deps=[h_early], inc=vc)
        h_bev2 = [None]
        h_p2a = [None]
        h_p2b = [None]
        h_eps = E("vector", lambda e: e.memset(epsc[:, :], 1e-5), inc=vc)
        h_MT0 = E("vector", lambda e: e.memset(MT[:, :], 0.0), inc=vc)
        h_MTb = E("vector", lambda e: e.memset(MTbs[:, :], 0.0), deps=[h_MT0], inc=vc)
        h_dMb0 = E("vector", lambda e: e.memset(dMb[:, :], 0.0), deps=[h_MTb], inc=vc)

        prev = None
        for (dst, src_) in ((cWq, wqT), (cWk, wkT), (cWv, wvT)):
            for h2 in range(2):
                hl = None
                for dd in range(4):
                    d = h2 * 4 + dd
                    hl = E("sync", lambda e, s2=src_, d2=d, dd2=dd: e.dma_start(
                        out=wst1[:, dd2 * 256:(dd2 + 1) * 256],
                        in_=s2[d2 * 128:(d2 + 1) * 128, :]),
                        deps=[prev] if dd == 0 else [], inc=lw1, inc_n=16)
                prev = E("vector", lambda e, d2=dst, h3=h2: e.tensor_copy(
                    d2[:, h3 * 1024:(h3 + 1) * 1024], wst1[:, 0:1024]), deps=[hl], inc=vc)
        for h2 in range(2):
            hl = E("sync", lambda e, h3=h2: e.dma_start(
                out=wst1[:, 0:1024], in_=woT[h3 * 128:(h3 + 1) * 128, :]),
                deps=[prev], inc=lw1, inc_n=16)
            prev = E("vector", lambda e, h3=h2: e.tensor_copy(
                cWo[:, h3 * 1024:(h3 + 1) * 1024], wst1[:, 0:1024]), deps=[hl], inc=vc)
        h_smallw = prev

        for (dst, src_) in ((ones128f, ones128_d), (dvvc, dvv_d), (pwc, pw_d),
                            (onescf, onescol_d), (b1rc, b1r)):
            E("sync", lambda e, d2=dst, s2=src_: e.dma_start(out=d2[:, :], in_=s2[:, :]),
              inc=lc, inc_n=16)
        h_consts = (lc, lc.count)

        h_sg = E("scalar", lambda e: e.activation(sgs[:, :], sgf[:, :], AF.Sigmoid),
                 deps=[h_early], inc=sc)
        h_gz = E("vector", lambda e: e.memset(gtmp[:, :], 0.0), inc=vc)
        h_gc = E("vector", lambda e: e.tensor_copy(gtmp[0:1, :], sgs[:, :]),
                 deps=[h_sg, h_gz], inc=vc)
        h_gmm = E("tensor", lambda e: e.matmul(pR[:, 448:449], ones128f[:, :], gtmp[:, :],
                                               start=True, stop=True),
                  deps=[h_gc, h_consts], inc=tn)
        h_gbc = E("vector", lambda e: e.tensor_copy(gateb[:, :], pR[:, 448:449]),
                  deps=[h_gmm], inc=vc)
        h_u = E("vector", lambda e: e.tensor_tensor(uvec[:, :], gateb[:, :], dq2c[:, :],
                                                    op=ALU.mult), deps=[h_gbc, h_early], inc=vc)
        shard_pieces = []
        for j in range(8):
            shard_pieces.append(w1s[:, j * 512:(j + 1) * 512])
        for j in range(8):
            shard_pieces.append(w2s[(j // 2) * 128:(j // 2 + 1) * 128,
                                    (j % 2) * 512:(j % 2 + 1) * 512])
        h_shr = [None] * 16

        def emit_shard_piece(j):
            buf = agt if j % 2 == 0 else wst0
            hl = E("gpsimd", lambda e, a=shard_pieces[j], b2=buf: e.dma_start(
                out=b2[:, 0:512], in_=a),
                deps=[h_shr[j - 2] if j >= 2 else (h_idc if j == 1 else None)],
                inc=(lsh0, lsh1)[j % 2], inc_n=16)
            h_shr[j] = E("vector", lambda e, j2=j, b2=buf: e.reduce_sum(
                asum[:, j2:j2 + 1], b2[:, 0:512], axis=AX.X,
                apply_absolute_value=True), deps=[hl], inc=vc)

        h_terns = {}

        def emit_tern_piece(j):
            if j < 4:
                src_ap = w1s[:, j * 1024:(j + 1) * 1024]
                col = 0
            else:
                src_ap = w2s[(j - 4) * 128:(j - 3) * 128, :]
                col = 1
            buf = wst[j % 2]
            stg = wtt0
            hl = E("sync", lambda e, a=src_ap, b2=buf: e.dma_start(out=b2[:, :], in_=a),
                   deps=[h_terns.get(("cons", j - 2), st["nthr"]),
                         h_shr[15] if j <= 1 else None], inc=lw[j % 2], inc_n=16)
            hs1 = E("scalar", lambda e, b2=buf, c=col: e.activation(
                sgn1[:, :], b2[:, :], AF.Sign, bias=nthrb[:, c:c + 1]),
                deps=[hl, st["thrbc"], st["nthr"], h_terns.get(("comb", j - 1)),
                      h_terns.get(("gd", j - 2))], inc=sc)
            hs2 = E("scalar", lambda e, b2=buf, c=col: e.activation(
                sgn2[:, :], b2[:, :], AF.Sign, bias=thrb[:, c:c + 1]),
                deps=[hl, hs1], inc=sc)
            hc = E("gpsimd", lambda e, s2=stg: e.tensor_tensor(
                s2[:, :], sgn1[:, :], sgn2[:, :], op=ALU.add),
                deps=[hs2, h_terns.get(("gd", j - 1))], inc=gp)
            h_terns[("comb", j)] = hc
            h_terns[("cons", j)] = hs2
            h_terns[("gd", j)] = E("gpsimd", lambda e, j2=j, s2=stg: e.dma_start(
                out=gin3[j2, :, :], in_=s2[:, :]),
                deps=[hc, h_terns.get(("gd", j - 1))], inc=lwt, inc_n=16)

        def emit_w_readback(which, h_agw):
            hrb = None
            for cc in range(NC):
                if which == 0:
                    dst = w1t[:, cc * FF:(cc + 1) * FF].rearrange("p (s d) -> p s d", s=4)
                    srcap = gout3[8 * cc:8 * cc + 4, :, :].rearrange("s p d -> p s d")
                else:
                    dst = w2t[:, 4 * cc * DM:(4 * cc + 4) * DM].rearrange(
                        "p (s d) -> p s d", s=4)
                    srcap = gout3[8 * cc + 4:8 * cc + 8, :, :].rearrange("s p d -> p s d")
                hrb = E("scalar", lambda e, d2=dst, s2=srcap: e.dma_start(
                    out=d2, in_=s2),
                    deps=[h_agw] if cc == 0 else [], inc=(lwr1, lwr2)[which], inc_n=16)
            return hrb

        h_xn = [None] * NT
        h_xnTe = [None] * NT
        h_qds = [None] * NT
        h_ks = [None] * NT
        h_vs = [None] * NT
        h_vds = [None] * NT
        h_qdTe = [None] * NT
        h_kTe = [None] * NT
        h_P = [None] * NT
        h_rine = [None] * NT
        h_Mupd = [None] * NT
        h_MTbs = [None] * NT
        h_dMb = [None] * NT
        h_rmms = [None] * NT

        st = {"totc": None, "thr": None, "thrbc": None, "nthr": None,
              "wrb1": None, "wrb2": None}
        h_agw0 = [None]

        dCf = float(np.float64(DEC) ** TS)
        h_txp = [None]

        def emit_LN(i):
            p = i % 2
            if i >= 2:
                for bb in range(B):
                    h_xld[i] = E("sync", lambda e, i2=i, p2=p, b2=bb: e.dma_start(
                        out=xst[p2][b2 * TS:(b2 + 1) * TS, :],
                        in_=xc[b2, i2 * TS:(i2 + 1) * TS, :]),
                        deps=[h_xn[i - 2]] if bb == 0 else [],
                        inc=lx[p], inc_n=16)
            h_bn1 = E("vector", lambda e, p2=p: e.bn_stats(
                lnbn[:, 12 * p2:12 * p2 + 6], xst[p2][:, 0:512]),
                deps=[h_xld[i], h_xn[i - 2] if i >= 2 else None], inc=vc)
            h_bn2 = E("vector", lambda e, p2=p: e.bn_stats(
                lnbn[:, 12 * p2 + 6:12 * p2 + 12], xst[p2][:, 512:1024]),
                deps=[h_xld[i]], inc=vc)
            h_agr = E("vector", lambda e, p2=p: e.bn_aggr(
                lnagg[:, 2 * p2:2 * p2 + 2], lnbn[:, 12 * p2:12 * p2 + 12]),
                deps=[h_bn1, h_bn2], inc=vc)
            h_std = E("scalar", lambda e, p2=p: e.activation(
                lnstd[:, p2:p2 + 1], lnagg[:, 2 * p2 + 1:2 * p2 + 2], AF.Sqrt,
                bias=epsc[:, :]), deps=[h_agr, h_eps], inc=sc)
            h_rs = E("vector", lambda e, p2=p: e.reciprocal(
                lnrs[:, p2:p2 + 1], lnstd[:, p2:p2 + 1]),
                deps=[h_std, h_xn[i - 2] if i >= 2 else None], inc=vc)
            h_mb2s[i] = E("vector", lambda e, p2=p: e.tensor_scalar(
                lnmb[:, p2:p2 + 1], lnagg[:, 2 * p2:2 * p2 + 1], lnrs[:, p2:p2 + 1],
                -1.0, op0=ALU.mult, op1=ALU.mult), deps=[h_rs], inc=vc)

        def emit_TQ(i):
            p = i % 2
            h_xn[i] = E("scalar", lambda e, p2=p: e.activation(
                xnt_t[:, p2 * DM:(p2 + 1) * DM], xst[p2][:, :], AF.Identity,
                bias=lnmb[:, p2:p2 + 1], scale=lnrs[:, p2:p2 + 1]),
                deps=[h_mb2s[i], h_xn[i - 2] if i >= 2 else None,
                      h_txs[i - 2] if i >= 2 else None], inc=sc)
            for d in range(8):
                E("tensor", lambda e, p2=p, d2=d: e.transpose(
                    pT[p2][:, d2 * 128:(d2 + 1) * 128],
                    xnt_t[:, p2 * DM + d2 * 128:p2 * DM + (d2 + 1) * 128],
                    identb[:, :]),
                  deps=[h_xn[i], h_idc, h_qdTe[i - 2] if i >= 2 else None,
                        h_kTe[i - 2] if i >= 2 else None],
                  inc=(tn if d == 7 else None))
            h_tx = (tn, tn.count)
            h_txs[i] = h_tx
            h_xnTe[i] = E("vector", lambda e, p2=p, i2=i: e.tensor_copy(
                xnT[:, :].rearrange("q (d g) -> q d g", d=8)[:, :, i2 * 128:(i2 + 1) * 128],
                pT[p2][:, :].rearrange("q (d g) -> q d g", d=8)),
                deps=[h_tx], inc=vc)
            pQ = (pQK, pH0)[p]
            pV = (pVS, pH1)[p]
            ql = p * 256
            qk_prev = [h_qds[i - 2] if i >= 2 else None, h_ks[i - 2] if i >= 2 else None]
            for (dst_sl, w) in (((0, 256), cWq), ((256, 512), cWk)):
                for d in range(8):
                    E("tensor", lambda e, d2=d, sl2=dst_sl, w2=w, i2=i, pq=pQ: e.matmul(
                        pq[:, sl2[0]:sl2[1]],
                        xnT[:, d2 * 2048 + i2 * 128:d2 * 2048 + (i2 + 1) * 128],
                        w2[:, d2 * 256:(d2 + 1) * 256],
                        start=(d2 == 0), stop=(d2 == 7)),
                      deps=[h_xnTe[i], h_smallw] + qk_prev, inc=(tn if d == 7 else None))
            h_kmm = (tn, tn.count)
            for d in range(8):
                E("tensor", lambda e, d2=d, i2=i, pv=pV: e.matmul(
                    pv[:, 0:256],
                    xnT[:, d2 * 2048 + i2 * 128:d2 * 2048 + (i2 + 1) * 128],
                    cWv[:, d2 * 256:(d2 + 1) * 256],
                    start=(d2 == 0), stop=(d2 == 7)),
                  deps=[h_xnTe[i], h_smallw,
                        h_vs[i - 2] if i >= 2 else None, h_vds[i - 2] if i >= 2 else None,
                        h_P[i - 2] if i >= 2 else None],
                  inc=(tn if d == 7 else None))
            h_vmm = (tn, tn.count)
            h_qds[i] = E("scalar", lambda e, pq=pQ, q2=ql: e.activation(
                qd_s[:, q2:q2 + 256], pq[:, 0:256], AF.Identity, scale=uvec[:, :]),
                deps=[h_kmm, h_u], inc=sc)
            q4 = (i % 4) * 256
            h_ks[i] = E("scalar", lambda e, pq=pQ, q2=q4: e.copy(
                k_s[:, q2:q2 + 256], pq[:, 256:512]), deps=[h_kmm], inc=sc)
            h_vs[i] = E("scalar", lambda e, pv=pV, q2=q4: e.copy(
                v_s[:, q2:q2 + 256], pv[:, 0:256]), deps=[h_vmm], inc=sc)
            h_vds[i] = E("scalar", lambda e, pv=pV, q2=q4: e.activation(
                vd_s[:, q2:q2 + 256], pv[:, 0:256], AF.Identity, scale=dvvc[:, :]),
                deps=[h_vmm], inc=sc)
            for (g, src_, coff, qq2) in ((0, qd_s, 0, ql), (1, qd_s, 128, ql),
                                         (0, k_s, 256, q4), (1, k_s, 384, q4)):
                E("tensor", lambda e, p2=p, g2=g, s2=src_, c2=coff, q2=qq2: e.transpose(
                    pT[p2][:, 512 + c2:512 + c2 + 128],
                    s2[:, q2 + g2 * 128:q2 + (g2 + 1) * 128], identb[:, :]),
                  deps=[h_qds[i], h_ks[i], h_xnTe[i]],
                  inc=(tn if coff == 384 else None))
            h_t2 = (tn, tn.count)
            h_qdTe[i] = E("scalar", lambda e, p2=p, i2=i: e.copy(
                qdT[:, i2 * 256:(i2 + 1) * 256], pT[p2][:, 512:768]), deps=[h_t2], inc=sc)
            h_kTe[i] = E("scalar", lambda e, p2=p, q2=ql: e.copy(
                kT_s[:, q2:q2 + 256], pT[p2][:, 768:1024]), deps=[h_t2], inc=sc)

        def emit_AT1(i):
            p = i % 2
            pV = (pVS, pH1)[p]
            ql = p * 256
            pl = p * 128
            for g in range(2):
                E("tensor", lambda e, g2=g, i2=i, pv=pV, q2=ql: e.matmul(
                    pv[:, 256:384], kT_s[:, q2 + g2 * 128:q2 + (g2 + 1) * 128],
                    qdT[:, i2 * 256 + g2 * 128:i2 * 256 + (g2 + 1) * 128],
                    start=(g2 == 0), stop=(g2 == 1)),
                  deps=[h_qdTe[i], h_kTe[i], h_vs[i], h_vds[i]], inc=(tn if g == 1 else None))
            h_smm = (tn, tn.count)
            h_P[i] = E("vector", lambda e, pv=pV, p3=pl: e.tensor_tensor(
                ptb[:, p3:p3 + 128], pv[:, 256:384], dmaskc[:, :], op=ALU.mult),
                deps=[h_smm], inc=vc)

        def emit_AT2(i):
            p = i % 2
            ql = p * 256
            q4 = (i % 4) * 256
            pl = p * 128
            retr_deps = [h_P[i], h_vs[i], h_qdTe[i],
                         h_MTbs[i - 2] if i >= 2 else h_MTb,
                         h_dMb[i - 1] if i >= 1 else h_dMb0,
                         h_rine[i - 1] if i >= 1 else h_gbc]
            if i == 3:
                retr_deps.append(h_bev2[0])
            if i == 5:
                retr_deps.append(st["totc"])
            if i == 11:
                retr_deps.append(st["thrbc"])
            for h2 in range(2):
                E("tensor", lambda e, h3=h2, q2=q4, p3=pl: e.matmul(
                    pR[:, h3 * 128:(h3 + 1) * 128],
                    v_s[:, q2 + h3 * 128:q2 + (h3 + 1) * 128], ptb[:, p3:p3 + 128],
                    start=True, stop=False), deps=retr_deps, inc=None)
                for g in range(2):
                    E("tensor", lambda e, h3=h2, g2=g, i2=i, mo=(p * 512): e.matmul(
                        pR[:, h3 * 128:(h3 + 1) * 128],
                        MTbs[:, mo + g2 * 256 + h3 * 128:mo + g2 * 256 + (h3 + 1) * 128],
                        qdT[:, i2 * 256 + g2 * 128:i2 * 256 + (g2 + 1) * 128],
                        start=False, stop=False),
                      deps=retr_deps, inc=None)
                for g in range(2):
                    E("tensor", lambda e, h3=h2, g2=g, i2=i, mo=(((i - 1) % 2) * 512): e.matmul(
                        pR[:, h3 * 128:(h3 + 1) * 128],
                        dMb[:, mo + g2 * 256 + h3 * 128:mo + g2 * 256 + (h3 + 1) * 128],
                        qdT[:, i2 * 256 + g2 * 128:i2 * 256 + (g2 + 1) * 128],
                        start=False, stop=(g2 == 1)),
                      deps=retr_deps, inc=(tn if (h2 == 1 and g == 1) else None))
            h_rmm = (tn, tn.count)
            h_rmms[i] = h_rmm
            h_rine[i] = E("vector", lambda e, i2=i: e.tensor_copy(
                rin[:, i2 * 256:(i2 + 1) * 256], pR[:, 0:256]), deps=[h_rmm], inc=vc)
            for g in range(2):
                E("tensor", lambda e, g2=g, q2=q4: e.matmul(
                    pM[:, g2 * 256:(g2 + 1) * 256],
                    k_s[:, q2 + g2 * 128:q2 + (g2 + 1) * 128], vd_s[:, q2:q2 + 256],
                    start=True, stop=True),
                  deps=[h_ks[i], h_vds[i], h_Mupd[i - 1] if i >= 1 else None,
                        h_dMb[i - 1] if i >= 1 else None, h_rmm],
                  inc=(tn if g == 1 else None))
            h_dMmm = (tn, tn.count)
            h_dMb[i] = E("vector", lambda e, mo=(p * 512): e.tensor_copy(
                dMb[:, mo:mo + 512], pM[:, :]),
                deps=[h_dMmm, h_rmms[i - 1] if i >= 1 else None], inc=vc)
            h_mts = E("vector", lambda e: e.tensor_scalar(MT[:, :], MT[:, :], dCf, None,
                                                          op0=ALU.mult), deps=[h_dMmm], inc=vc)
            h_Mupd[i] = E("vector", lambda e: e.tensor_tensor(MT[:, :], MT[:, :], pM[:, :],
                                                              op=ALU.add), deps=[h_dMmm, h_mts], inc=vc)
            h_MTbs[i] = E("vector", lambda e, mo=(p * 512): e.tensor_scalar(
                MTbs[:, mo:mo + 512], MT[:, :], dCf, None, op0=ALU.mult),
                deps=[h_Mupd[i], h_rmm], inc=vc)

        h_txs = [None] * NT
        h_mb2s = [None] * NT
        for it in range(NT + 4):
            if 2 <= it < NT + 2:
                emit_TQ(it - 2)
            if 3 <= it < NT + 3:
                emit_AT1(it - 3)
            if it < NT:
                emit_LN(it)
            if it < 4:
                continue
            i = it - 4
            emit_AT2(i)
            if i < 4:
                for j in range(4 * i, 4 * i + 4):
                    emit_shard_piece(j)
            if i == 2:
                h_wz = E("vector", lambda e: e.memset(wst1[:, :], 0.0),
                         deps=[h_smallw], inc=vc)
                h_b20 = E("sync", lambda e: e.dma_start(out=wst1[0:1, 0:DM], in_=b2r[:, :]),
                          deps=[h_wz], inc=lw1, inc_n=16)
                h_bmm1 = E("tensor", lambda e: e.matmul(
                    pR[:, 0:512], ones128f[:, :], wst1[:, 0:512], start=True, stop=True),
                    deps=[h_b20, h_wz, h_rine[2], h_consts], inc=tn)
                h_bev1 = E("vector", lambda e: e.tensor_copy(b2bc[:, 0:512], pR[:, 0:512]),
                           deps=[h_bmm1], inc=vc)
                h_bmm2 = E("tensor", lambda e: e.matmul(
                    pR[:, 0:512], ones128f[:, :], wst1[:, 512:1024], start=True, stop=True),
                    deps=[h_bev1], inc=tn)
                h_bev2[0] = E("vector", lambda e: e.tensor_copy(b2bc[:, 512:1024], pR[:, 0:512]),
                              deps=[h_bmm2], inc=vc)
            if i == 3:
                h_p2a[0] = E("vector", lambda e: e.reduce_sum(
                    part2[:, 0:1], asum[:, 0:8], axis=AX.X), deps=[h_shr[7]], inc=vc)
                h_p2b[0] = E("vector", lambda e: e.reduce_sum(
                    part2[:, 1:2], asum[:, 8:16], axis=AX.X), deps=[h_shr[15]], inc=vc)
            if i == 4:
                h_totmm = E("tensor", lambda e: e.matmul(
                    pR[0:1, 456:458], onescf[:, :], part2[:, :], start=True, stop=True),
                    deps=[h_p2a[0], h_p2b[0], h_consts, h_rine[4]], inc=tn)
                st["totc"] = E("vector", lambda e: e.tensor_copy(tot12[:, :], pR[0:1, 456:458]),
                               deps=[h_totmm], inc=vc)
                hg1 = E("gpsimd", lambda e: e.dma_start(out=gin2[:, :], in_=tot12[:, :]),
                        deps=[st["totc"]], inc=lgm, inc_n=16)
                h_ar = E("gpsimd", lambda e: e.collective_compute(
                    "AllReduce", ALU.add, replica_groups=[list(range(NC))],
                    ins=[gin2[:, :]], outs=[gout2[:, :]]), deps=[hg1], inc=gp)
                hg2 = E("gpsimd", lambda e: e.dma_start(out=tot12g[:, :], in_=gout2[:, :]),
                        deps=[h_ar], inc=lar, inc_n=16)
                st["thr"] = E("gpsimd", lambda e: e.tensor_scalar(
                    thr12[:, :], tot12g[:, :], 0.5 / NW, 0.5e-8, op0=ALU.mult, op1=ALU.max),
                    deps=[hg2], inc=gp)
            if i == 9:
                h_tz = E("gpsimd", lambda e: e.memset(ttmp[:, :], 0.0), inc=gp)
                h_tc = E("gpsimd", lambda e: e.tensor_copy(ttmp[0:1, :], thr12[:, :]),
                         deps=[st["thr"], h_tz], inc=gp)
            if i == 10:
                h_thrmm = E("tensor", lambda e: e.matmul(
                    pR[:, 458:460], ones128f[:, :], ttmp[:, :], start=True, stop=True),
                    deps=[h_tc, h_rine[10]], inc=tn)
                st["thrbc"] = E("scalar", lambda e: e.copy(thrb[:, :], pR[:, 458:460]),
                                deps=[h_thrmm], inc=sc)
                st["nthr"] = E("gpsimd", lambda e: e.tensor_scalar(
                    nthrb[:, :], thrb[:, :], -1.0, None, op0=ALU.mult),
                    deps=[st["thrbc"]], inc=gp)
            if i in (11, 12, 13, 14):
                emit_tern_piece(2 * (i - 11))
                emit_tern_piece(2 * (i - 11) + 1)
                if i == 14:
                    h_agw0[0] = E("gpsimd", lambda e: e.collective_compute(
                        "AllGather", ALU.bypass, replica_groups=[list(range(NC))],
                        ins=[gin3[:, :, :]], outs=[gout3[:, :, :]]),
                        deps=[h_terns[("gd", 7)]], inc=gp)

        hgm = []
        for g in range(2):
            hgm.append(E("gpsimd", lambda e, g2=g: e.dma_start(
                out=ginM[g2 * 128:(g2 + 1) * 128, :], in_=MT[:, g2 * 256:(g2 + 1) * 256]),
                deps=[h_Mupd[NT - 1]], inc=lgm, inc_n=16))
        h_ag = E("gpsimd", lambda e: e.collective_compute(
            "AllGather", ALU.bypass, replica_groups=[list(range(NC))],
            ins=[ginM[:, :]], outs=[goutM[:, :]]), deps=hgm, inc=gp)
        st["wrb1"] = emit_w_readback(0, h_agw0[0])
        st["wrb2"] = emit_w_readback(1, h_agw0[0])
        h_cmb = None
        for g in range(2):
            h_m0l = E("scalar", lambda e, g2=g: e.dma_start(
                out=agt[:, g2 * 256:(g2 + 1) * 256],
                in_=m0T[g2 * 128:(g2 + 1) * 128, :]),
                deps=[(lsh0, h_shr[15][1] if False else None)] if False else
                     ([h_shr[14], h_shr[15]] if g == 0 else []), inc=lag, inc_n=16)
        h_cmb = E("vector", lambda e: e.tensor_scalar(
            MsT[:, :], agt[:, :], pwc[:, 8:9], None, op0=ALU.mult),
            deps=[h_m0l, h_consts], inc=vc)
        for cp in range(NC):
            for g in range(2):
                hr = E("scalar", lambda e, c2=cp, g2=g: e.dma_start(
                    out=agt[:, g2 * 256:(g2 + 1) * 256],
                    in_=goutM[c2 * 256 + g2 * 128:c2 * 256 + (g2 + 1) * 128, :]),
                    deps=[h_ag, h_cmb] if g == 0 else [], inc=lag, inc_n=16)
            h_sc1 = E("vector", lambda e, c2=cp: e.tensor_scalar(
                yh[:, :], agt[:, :], pwc[:, c2:c2 + 1], None, op0=ALU.mult),
                deps=[hr], inc=vc)
            h_cmb = E("vector", lambda e: e.tensor_tensor(MsT[:, :], MsT[:, :], yh[:, :],
                                                          op=ALU.add), deps=[h_sc1], inc=vc)
        h_MsTb = E("vector", lambda e: e.tensor_copy(MsTb[:, :], MsT[:, :]), deps=[h_cmb], inc=vc)

        lacc0 = _Sem("lacc0", "dma", "gpsimd")
        lacc1 = _Sem("lacc1", "dma", "gpsimd")
        lacc0.handle = _st.enter_context(nc.semaphore("s_lacc0"))
        lacc1.handle = _st.enter_context(nc.semaphore("s_lacc1"))
        lacc = (lacc0, lacc1)
        yat = (yat0, yat1)
        h_aev = {}
        h_accD = {}
        h_pgprev = [h_rine[NT - 1]]

        def emit_attn_final(t):
            pg_deps = [h_MsTb, h_pgprev[0]]
            for h2 in range(2):
                for g in range(2):
                    E("tensor", lambda e, h3=h2, g2=g, t2=t: e.matmul(
                        pR[:, h3 * 128:(h3 + 1) * 128],
                        MsTb[:, g2 * 256 + h3 * 128:g2 * 256 + (h3 + 1) * 128],
                        qdT[:, t2 * 256 + g2 * 128:t2 * 256 + (g2 + 1) * 128],
                        start=(g2 == 0), stop=(g2 == 1)),
                      deps=pg_deps, inc=(tn if (h2 == 1 and g == 1) else None))
            h_pg = (tn, tn.count)
            h_r1 = E("vector", lambda e, t2=t: e.tensor_scalar(
                rtmp[:, :], pR[:, 0:256], float(np.float64(DEC) ** (TS * t2)), None,
                op0=ALU.mult), deps=[h_pg], inc=vc)
            h_pgprev[0] = h_r1
            h_rTs[t] = E("vector", lambda e, t2=t: e.tensor_tensor(
                rT_s[:, :], rtmp[:, :], rin[:, t2 * 256:(t2 + 1) * 256], op=ALU.add),
                deps=[h_r1, h_rTs[t - 1] if t >= 1 else None], inc=vc)
            for e2 in range(2):
                for h2 in range(2):
                    E("tensor", lambda e, e3=e2, h3=h2: e.matmul(
                        pM[:, 0:512], rT_s[:, h3 * 128:(h3 + 1) * 128],
                        cWo[:, h3 * 1024 + e3 * 512:h3 * 1024 + (e3 + 1) * 512],
                        start=(h3 == 0), stop=(h3 == 1)),
                      deps=[h_rTs[t], h_aev.get((t, 0)) if e2 == 1 else
                            (h_aev.get((t - 1, 1)) if t >= 1 else h_dMb[NT - 1]),
                            h_accD.get(t - 2)],
                      inc=(tn if h2 == 1 else None))
                h_pmm = (tn, tn.count)
                h_aev[(t, e2)] = E("vector", lambda e, t2=t, e3=e2: e.tensor_scalar(
                    yat[t2 % 2][:, e3 * 512:(e3 + 1) * 512], pM[:, 0:512], 0.5, None,
                    op0=ALU.mult), deps=[h_pmm, h_accD.get(t - 2)], inc=vc)
            hprev = h_accD.get(t - 2)
            for bb in range(B):
                hprev = E("gpsimd", lambda e, t2=t, b2=bb: e.dma_start(
                    out=yc[b2, t2 * TS:(t2 + 1) * TS, :],
                    in_=yat[t2 % 2][b2 * TS:(b2 + 1) * TS, :],
                    accum_op=ALU.add),
                    deps=[h_aev[(t, 0)], h_aev[(t, 1)], h_ydma[t], hprev]
                    if bb == 0 else [hprev], inc=lacc[t % 2], inc_n=16)
            h_accD[t] = hprev

        h_gelu = [None] * 32
        h_lastB = None
        h_yhev = {}
        h_yadd = {}
        h_ydma = [None] * NT
        h_xrr = [None] * NT
        h_rTs = [None] * NT
        prev_yh_user = h_MsTb
        for b in range(8):
            for f in range(32):
                a_deps = [st["wrb1"], h_gelu[(f - 2) % 32]]
                if b == 0 and f < 2:
                    a_deps += [h_qds[NT - 1], h_ks[NT - 1], h_vs[NT - 1],
                               h_vds[NT - 1], h_P[NT - 1]]
                for d in range(8):
                    E("tensor", lambda e, d2=d, f2=f, b2=b: e.matmul(
                        pH[f2 % 2][:, 0:256],
                        w1t[:, d2 * FF + f2 * 128:d2 * FF + (f2 + 1) * 128],
                        xnT[:, d2 * 2048 + b2 * 256:d2 * 2048 + (b2 + 1) * 256],
                        start=(d2 == 0), stop=(d2 == 7)),
                      deps=a_deps, inc=(tn if d == 7 else None))
                h_amm = (tn, tn.count)
                h_gelu[f] = E("scalar", lambda e, f2=f: e.activation(
                    hbuf[:, f2 * 256:(f2 + 1) * 256], pH[f2 % 2][:, 0:256], AF.Gelu,
                    scale=0.5, bias=b1rc[:, f2:f2 + 1]),
                    deps=[h_amm, h_consts, h_lastB], inc=sc)
            for tl in range(2):
                t = 2 * b + tl
                for bb in range(B):
                    h_xrr[t] = E("gpsimd", lambda e, t2=t, b2=bb: e.dma_start(
                        out=xst[t2 % 2][b2 * TS:(b2 + 1) * TS, :],
                        in_=xc[b2, t2 * TS:(t2 + 1) * TS, :]),
                        deps=([h_xn[NT - 1], h_yadd.get((t - 2, 1))]
                              if bb == 0 else []), inc=lrr[t % 2], inc_n=16)
                h_xrr[t] = E("gpsimd", lambda e, t2=t: e.tensor_tensor(
                    xst[t2 % 2][:, :], xst[t2 % 2][:, :], b2bc[:, :], op=ALU.add),
                    deps=[h_xrr[t], h_bev2[0]], inc=gp)
                grp_deps = [h_gelu[31], st["wrb2"],
                            h_yhev.get((t - 1, 0)), h_yhev.get((t - 1, 1)), h_smallw]
                if t == 0:
                    grp_deps += [h_qds[NT - 2], h_ks[NT - 2],
                                 h_vs[NT - 2], h_vds[NT - 2], h_P[NT - 2]]
                for f in range(32):
                    for e2 in range(2):
                        E("tensor", lambda e, e3=e2, f2=f, tl2=tl: e.matmul(
                            pO[e3][:, 0:512],
                            hbuf[:, f2 * 256 + tl2 * 128:f2 * 256 + (tl2 + 1) * 128],
                            w2t[:, f2 * DM + e3 * 512:f2 * DM + (e3 + 1) * 512],
                            start=(f2 == 0), stop=(f2 == 31)),
                          deps=grp_deps,
                          inc=(tn if (f == 31 and e2 == 1) else None))
                h_bmm = (tn, tn.count)
                if tl == 1:
                    h_lastB = h_bmm
                for e2 in range(2):
                    h_yhev[(t, e2)] = E("scalar", lambda e, e3=e2: e.activation(
                        yh[:, :], pO[e3][:, 0:512], AF.Identity, scale=0.5),
                        deps=[h_bmm, prev_yh_user], inc=sc)
                    h_yadd[(t, e2)] = E("vector", lambda e, t2=t, e3=e2: e.tensor_tensor(
                        ybuf[:, e3 * 512:(e3 + 1) * 512], yh[:, :],
                        xst[t2 % 2][:, e3 * 512:(e3 + 1) * 512], op=ALU.add),
                        deps=[h_yhev[(t, e2)], h_xrr[t],
                              h_ydma[t - 1] if t >= 1 else None], inc=vc)
                    prev_yh_user = h_yadd[(t, e2)]
                for bb in range(B):
                    h_ydma[t] = E("gpsimd", lambda e, t2=t, b2=bb: e.dma_start(
                        out=yc[b2, t2 * TS:(t2 + 1) * TS, :],
                        in_=ybuf[b2 * TS:(b2 + 1) * TS, :]),
                        deps=([h_yadd[(t, 0)], h_yadd[(t, 1)]] if bb == 0 else []),
                        inc=lyo, inc_n=16)
            sched = {2: (0, 1), 3: (2, 3, 4), 4: (5, 6, 7), 5: (8, 9, 10),
                     6: (11, 12, 13), 7: (14, 15)}
            for t in sched.get(b, ()):
                emit_attn_final(t)
        E("gpsimd", None, deps=[h_ydma[NT - 1], (lyo, lyo.count),
                                (lacc0, lacc0.count), (lacc1, lacc1.count)])

        @block.sync
        def _(eng):
            P.run_engine("sync", eng)

        @block.gpsimd
        def _(eng):
            P.run_engine("gpsimd", eng)

        @block.tensor
        def _(eng):
            P.run_engine("tensor", eng)

        @block.vector
        def _(eng):
            P.run_engine("vector", eng)

        @block.scalar
        def _(eng):
            P.run_engine("scalar", eng)

    return nc


def _consts(core):
    sloc = np.arange(128) % TS
    d64 = np.float64(DEC)
    dmask = np.zeros((128, 128), np.float64)
    for j in range(128):
        for t in range(128):
            if sloc[j] < sloc[t]:
                dmask[j, t] = d64 ** (-1.0 - sloc[j]) / B
    pw = np.zeros((128, 9), np.float64)
    for cp in range(core):
        pw[:, cp] = d64 ** (SEG * (core - 1 - cp))
    pw[:, 8] = d64 ** (SEG * core)
    return {
        "identf": np.eye(128, dtype=np.float32),
        "ones128": np.ones((128, 128), np.float32),
        "dmaskf": dmask.astype(np.float32),
        "dq2": (2.0 * d64 ** sloc).reshape(128, 1).astype(np.float32),
        "dvv": (d64 ** (TS - 1 - sloc) / B).reshape(128, 1).astype(np.float32),
        "pw": pw.astype(np.float32),
        "onescol": np.ones((128, 1), np.float32),
    }


_NC_CACHE = [None]


def kernel(L0_attn_x, L0_attn_Wq, L0_attn_Wk, L0_attn_Wv, L0_attn_Wo, L0_attn_gate,
           L0_attn_M0, L0_ffn_W1, L0_ffn_b1, L0_ffn_W2, L0_ffn_b2):
    x = np.ascontiguousarray(np.asarray(L0_attn_x, dtype=np.float32))
    wqT = np.ascontiguousarray(np.asarray(L0_attn_Wq, np.float32).T)
    wkT = np.ascontiguousarray(np.asarray(L0_attn_Wk, np.float32).T)
    wvT = np.ascontiguousarray(np.asarray(L0_attn_Wv, np.float32).T)
    woT = np.ascontiguousarray(np.asarray(L0_attn_Wo, np.float32).T)
    w1T = np.ascontiguousarray(np.asarray(L0_ffn_W1, np.float32).T)
    w2T = np.ascontiguousarray(np.asarray(L0_ffn_W2, np.float32).T)
    b1r = np.ascontiguousarray(np.asarray(L0_ffn_b1, np.float32).reshape(32, 128).T)
    b2r = np.ascontiguousarray(np.asarray(L0_ffn_b2, np.float32).reshape(1, DM))
    gate = np.asarray(L0_attn_gate, np.float32).reshape(1, 1)
    m0T = np.ascontiguousarray(np.asarray(L0_attn_M0, np.float32).T)

    if _NC_CACHE[0] is None:
        _NC_CACHE[0] = build_program()
    nc = _NC_CACHE[0]

    in_maps = []
    for c in range(NC):
        m = {
            "xc": np.ascontiguousarray(x[:, c * SEG:(c + 1) * SEG, :]),
            "wqT": wqT, "wkT": wkT, "wvT": wvT, "woT": woT,
            "w1s": np.ascontiguousarray(w1T[c * 128:(c + 1) * 128, :]),
            "w2s": np.ascontiguousarray(w2T[c * 512:(c + 1) * 512, :]),
            "b1r": b1r, "b2r": b2r, "gatei": gate, "m0T": m0T,
        }
        m.update(_consts(c))
        in_maps.append(m)
    res = run_bass_kernel_spmd(nc, in_maps, core_ids=list(range(NC)))
    out = np.concatenate([np.asarray(r["yc"]) for r in res.results], axis=1)
    return out


# version 15
# speedup vs baseline: 1.0557x; 1.0557x over previous
import numpy as np
import concourse.bass as bass
import concourse.mybir as mybir
from concourse.bass_utils import run_bass_kernel_spmd

F32 = mybir.dt.float32
BF16 = mybir.dt.bfloat16
FP8 = mybir.dt.float8e4
AF = mybir.ActivationFunctionType
ALU = mybir.AluOpType
AX = mybir.AxisListType

B = 4
S = 4096
DM = 1024
MEM = 256
FF = 4096
TS = 32
NC = 8
SEG = S // NC
NT = SEG // TS
DEC = 0.99
NW = DM * FF
W2_SCALAR = 20


class _Sem:
    def __init__(self, name, kind, owner):
        self.name = name
        self.kind = kind
        self.owner = owner
        self.count = 0
        self.handle = None


class _Prog:
    ENGS = ("sync", "gpsimd", "tensor", "vector", "scalar")

    def __init__(self):
        self.items = {e: [] for e in self.ENGS}

    def emit(self, eng, fn, deps=(), inc=None, inc_n=1):
        deps = tuple(d for d in deps if d is not None)
        if eng == "gpsimd" and inc is not None and inc.kind == "eng" and inc.count:
            deps = deps + ((inc, inc.count),)
        post = None
        if inc is not None:
            inc.count += inc_n
            post = (inc, inc.count)
        self.items[eng].append((fn, deps, inc, inc_n))
        return post

    def run_engine(self, engname, engobj):
        last = {}
        for fn, deps, inc, inc_n in self.items[engname]:
            need = {}
            for sem, val in deps:
                if val > need.get(sem, -1):
                    need[sem] = val
            for sem, val in need.items():
                if last.get(sem, -1) >= val:
                    continue
                engobj.wait_ge(sem.handle, val)
                last[sem] = val
            if fn is None:
                continue
            instr = fn(engobj)
            if inc is not None:
                instr.then_inc(inc.handle, inc_n)


def build_program():
    nc = bass.Bass("TRN2", target_bir_lowering=False, debug=False)
    P = _Prog()

    def dram_in(name, shape):
        return nc.dram_tensor(name, shape, F32, kind="ExternalInput").ap()

    xc = dram_in("xc", [B, SEG, DM])
    wqT = dram_in("wqT", [DM, MEM])
    wkT = dram_in("wkT", [DM, MEM])
    wvT = dram_in("wvT", [DM, MEM])
    woT = dram_in("woT", [MEM, DM])
    w1s = dram_in("w1s", [128, FF])
    w2s = dram_in("w2s", [512, DM])
    b1r = dram_in("b1r", [128, FF // 128])
    b2r = dram_in("b2r", [128, DM])
    gatei = dram_in("gatei", [1, 1])
    m0T = dram_in("m0T", [MEM, MEM])
    identf = dram_in("identf", [128, 128])
    ones128_d = dram_in("ones128", [128, 128])
    dmaskf_d = dram_in("dmaskf", [128, 128])
    dq2_d = dram_in("dq2", [128, 1])
    dvv_d = dram_in("dvv", [128, 1])
    pw_d = dram_in("pw", [128, 9])
    onescol_d = dram_in("onescol", [128, 1])
    yc = nc.dram_tensor("yc", [B, SEG, DM], F32, kind="ExternalOutput").ap()
    gin2 = nc.dram_tensor("gin2", [1, 2], F32).ap()
    gout2 = nc.dram_tensor("gout2", [1, 2], F32).ap()
    ginM = nc.dram_tensor("ginM", [MEM, MEM], F32).ap()
    goutM = nc.dram_tensor("goutM", [NC * MEM, MEM], F32).ap()
    gin3 = nc.dram_tensor("gin3", [8, 128, 1024], FP8).ap()
    gout3 = nc.dram_tensor("gout3", [64, 128, 1024], FP8).ap()

    lc = _Sem("lc", "dma", "sync")
    lw0 = _Sem("lw0", "dma", "sync")
    lw1 = _Sem("lw1", "dma", "sync")
    lx0 = _Sem("lx0", "dma", "sync")
    lx1 = _Sem("lx1", "dma", "sync")
    lyo = _Sem("lyo", "dma", "gpsimd")
    lgm = _Sem("lgm", "dma", "gpsimd")
    lag = _Sem("lag", "dma", "scalar")
    lar = _Sem("lar", "dma", "sync")
    lrr0 = _Sem("lrr0", "dma", "gpsimd")
    lrr1 = _Sem("lrr1", "dma", "gpsimd")
    lsh0 = _Sem("lsh0", "dma", "gpsimd")
    lsh1 = _Sem("lsh1", "dma", "gpsimd")
    lwt = _Sem("lwt", "dma", "gpsimd")
    lwr1 = _Sem("lwr1", "dma", "scalar")
    lwr2 = _Sem("lwr2", "dma", "scalar")
    tn = _Sem("tn", "eng", "tensor")
    vc = _Sem("vc", "eng", "vector")
    sc = _Sem("sc", "eng", "scalar")
    gp = _Sem("gp", "eng", "gpsimd")
    lx = (lx0, lx1)
    lw = (lw0, lw1)
    lrr = (lrr0, lrr1)

    import contextlib
    with contextlib.ExitStack() as _st:
        def sb(name, shape, dt):
            return _st.enter_context(nc.sbuf_tensor(name, shape, dt))

        def ps(name, shape, dt):
            return _st.enter_context(nc.psum_tensor(name, shape, dt))

        w1t = sb("w1t", [128, 8 * FF], FP8)
        w2t = sb("w2t", [128, 32 * DM], FP8)
        cWq = sb("cWq", [128, 8 * MEM], BF16)
        cWk = sb("cWk", [128, 8 * MEM], BF16)
        cWv = sb("cWv", [128, 8 * MEM], BF16)
        cWo = sb("cWo", [128, 2 * DM], BF16)
        xnT = sb("xnT", [128, 8 * (NT * 128)], BF16)
        qdT = sb("qdT", [128, NT * MEM], BF16)
        rin = sb("rin", [128, NT * MEM], BF16)
        hbuf = sb("hbuf", [128, 32 * 256], BF16)
        xst0 = sb("xst0", [128, DM], F32)
        xst1 = sb("xst1", [128, DM], F32)
        wst0 = sb("wst0", [128, 1024], F32)
        wst1 = sb("wst1", [128, 1024], F32)
        sgn1 = sb("sgn1", [128, 1024], BF16)
        sgn2 = sb("sgn2", [128, 1024], BF16)
        xnt_t = sb("xnt_t", [128, 2 * DM], BF16)
        qd_s = sb("qd_s", [128, 2 * MEM], BF16)
        k_s = sb("k_s", [128, 4 * MEM], BF16)
        v_s = sb("v_s", [128, 4 * MEM], BF16)
        vd_s = sb("vd_s", [128, 4 * MEM], BF16)
        kT_s = sb("kT_s", [128, 2 * MEM], BF16)
        rT_s = sb("rT_s", [128, MEM], BF16)
        ptb = sb("ptb", [128, 2 * 128], BF16)
        rtmp = sb("rtmp", [128, MEM], BF16)
        MT = sb("MT", [128, 512], F32)
        MTbs = sb("MTbs", [128, 2 * 512], BF16)
        dMb = sb("dMb", [128, 2 * 512], BF16)
        MsT = sb("MsT", [128, 512], F32)
        MsTb = sb("MsTb", [128, 512], BF16)
        agt = sb("agt", [128, 512], F32)
        ybuf = sb("ybuf", [128, DM], F32)
        yh = sb("yh", [128, 512], F32)
        dmaskc = sb("dmaskc", [128, 128], F32)
        identb = sb("identb", [128, 128], BF16)
        ones128f = sb("ones128f", [128, 128], F32)
        b2bc = sb("b2bc", [128, DM], BF16)
        dq2c = sb("dq2c", [128, 1], F32)
        dvvc = sb("dvvc", [128, 1], F32)
        pwc = sb("pwc", [128, 9], F32)
        onescf = sb("onescf", [128, 1], F32)
        b1rc = sb("b1rc", [128, 32], F32)
        sgf = sb("sgf", [1, 1], F32)
        sgs = sb("sgs", [1, 1], F32)
        gtmp = sb("gtmp", [128, 1], F32)
        wtt0 = sb("wtt0", [128, 1024], FP8)
        yat0 = sb("yat0", [128, DM], F32)
        yat1 = sb("yat1", [128, DM], F32)
        ttmp = sb("ttmp", [128, 2], F32)
        gateb = sb("gateb", [128, 1], F32)
        uvec = sb("uvec", [128, 1], F32)
        asum = sb("asum", [128, 16], F32)
        part2 = sb("part2", [128, 2], F32)
        tot12 = sb("tot12", [1, 2], F32)
        tot12g = sb("tot12g", [1, 2], F32)
        thr12 = sb("thr12", [1, 2], F32)
        thrb = sb("thrb", [128, 2], F32)
        nthrb = sb("nthrb", [128, 2], F32)
        lnbn = sb("lnbn", [128, 24], F32)
        lnagg = sb("lnagg", [128, 4], F32)
        lnstd = sb("lnstd", [128, 2], F32)
        lnrs = sb("lnrs", [128, 2], F32)
        lnmb = sb("lnmb", [128, 2], F32)
        epsc = sb("epsc", [128, 1], F32)
        pT0 = ps("pT0", [128, 1024], BF16)
        pT1 = ps("pT1", [128, 1024], BF16)
        pQK = ps("pQK", [128, 512], F32)
        pVS = ps("pVS", [128, 512], F32)
        pR = ps("pR", [128, 512], F32)
        pM = ps("pM", [128, 512], F32)
        pH0 = ps("pH0", [128, 512], F32)
        pH1 = ps("pH1", [128, 512], F32)
        for s in (lc, lw0, lw1, lx0, lx1, lyo, lgm, lag, lar, lrr0, lrr1, lsh0, lsh1, lwt, lwr1, lwr2, tn, vc, sc, gp):
            s.handle = _st.enter_context(nc.semaphore("s_" + s.name))
        block = _st.enter_context(nc.Block())

        pT = (pT0, pT1)
        xst = (xst0, xst1)
        wst = (wst0, wst1)
        pH = (pH0, pH1)
        pO = (pQK, pVS)
        E = P.emit

        E("sync", lambda e: e.dma_start(out=wst0[:, 0:128], in_=identf[:, :]),
          inc=lc, inc_n=16)
        for (dst, src_) in ((sgf, gatei), (dmaskc, dmaskf_d), (dq2c, dq2_d)):
            E("sync", lambda e, d2=dst, s2=src_: e.dma_start(out=d2[:, :], in_=s2[:, :]),
              inc=lc, inc_n=16)
        h_early = (lc, lc.count)
        h_xld = [None] * NT
        for i in range(2):
            for bb in range(B):
                h_xld[i] = E("sync", lambda e, i2=i, b2=bb: e.dma_start(
                    out=xst[i2][b2 * TS:(b2 + 1) * TS, :],
                    in_=xc[b2, i2 * TS:(i2 + 1) * TS, :]), inc=lx[i], inc_n=16)
        h_idc = E("vector", lambda e: e.tensor_copy(identb[:, :], wst0[:, 0:128]),
                  deps=[h_early], inc=vc)
        h_b2l = E("sync", lambda e: e.dma_start(out=wst1[:, :], in_=b2r[:, :]),
                  inc=lw1, inc_n=16)
        h_bev2 = [E("vector", lambda e: e.tensor_copy(b2bc[:, :], wst1[:, :]),
                   deps=[h_b2l], inc=vc)]
        h_p2a = [None]
        h_p2b = [None]
        h_eps = E("vector", lambda e: e.memset(epsc[:, :], 1e-5), inc=vc)
        h_MT0 = E("vector", lambda e: e.memset(MT[:, :], 0.0), inc=vc)
        h_MTb = E("vector", lambda e: e.memset(MTbs[:, :], 0.0), deps=[h_MT0], inc=vc)
        h_dMb0 = E("vector", lambda e: e.memset(dMb[:, :], 0.0), deps=[h_MTb], inc=vc)

        prev = h_bev2[0]
        for (dst, src_) in ((cWq, wqT), (cWk, wkT), (cWv, wvT)):
            for h2 in range(2):
                hl = None
                for dd in range(4):
                    d = h2 * 4 + dd
                    hl = E("sync", lambda e, s2=src_, d2=d, dd2=dd: e.dma_start(
                        out=wst1[:, dd2 * 256:(dd2 + 1) * 256],
                        in_=s2[d2 * 128:(d2 + 1) * 128, :]),
                        deps=[prev] if dd == 0 else [], inc=lw1, inc_n=16)
                prev = E("vector", lambda e, d2=dst, h3=h2: e.tensor_copy(
                    d2[:, h3 * 1024:(h3 + 1) * 1024], wst1[:, 0:1024]), deps=[hl], inc=vc)
        for h2 in range(2):
            hl = E("sync", lambda e, h3=h2: e.dma_start(
                out=wst1[:, 0:1024], in_=woT[h3 * 128:(h3 + 1) * 128, :]),
                deps=[prev], inc=lw1, inc_n=16)
            prev = E("vector", lambda e, h3=h2: e.tensor_copy(
                cWo[:, h3 * 1024:(h3 + 1) * 1024], wst1[:, 0:1024]), deps=[hl], inc=vc)
        h_smallw = prev

        for (dst, src_) in ((ones128f, ones128_d), (dvvc, dvv_d), (pwc, pw_d),
                            (onescf, onescol_d), (b1rc, b1r)):
            E("sync", lambda e, d2=dst, s2=src_: e.dma_start(out=d2[:, :], in_=s2[:, :]),
              inc=lc, inc_n=16)
        h_consts = (lc, lc.count)

        h_sg = E("scalar", lambda e: e.activation(sgs[:, :], sgf[:, :], AF.Sigmoid),
                 deps=[h_early], inc=sc)
        h_gz = E("vector", lambda e: e.memset(gtmp[:, :], 0.0), inc=vc)
        h_gc = E("vector", lambda e: e.tensor_copy(gtmp[0:1, :], sgs[:, :]),
                 deps=[h_sg, h_gz], inc=vc)
        h_gmm = E("tensor", lambda e: e.matmul(pR[:, 448:449], ones128f[:, :], gtmp[:, :],
                                               start=True, stop=True),
                  deps=[h_gc, h_consts], inc=tn)
        h_gbc = E("vector", lambda e: e.tensor_copy(gateb[:, :], pR[:, 448:449]),
                  deps=[h_gmm], inc=vc)
        h_u = E("vector", lambda e: e.tensor_tensor(uvec[:, :], gateb[:, :], dq2c[:, :],
                                                    op=ALU.mult), deps=[h_gbc, h_early], inc=vc)
        shard_pieces = []
        for j in range(8):
            shard_pieces.append(w1s[:, j * 512:(j + 1) * 512])
        for j in range(8):
            shard_pieces.append(w2s[(j // 2) * 128:(j // 2 + 1) * 128,
                                    (j % 2) * 512:(j % 2 + 1) * 512])
        h_shr = [None] * 16

        def emit_shard_piece(j):
            buf = agt if j % 2 == 0 else wst0
            hl = E("gpsimd", lambda e, a=shard_pieces[j], b2=buf: e.dma_start(
                out=b2[:, 0:512], in_=a),
                deps=[h_shr[j - 2] if j >= 2 else (h_idc if j == 1 else None)],
                inc=(lsh0, lsh1)[j % 2], inc_n=16)
            h_shr[j] = E("vector", lambda e, j2=j, b2=buf: e.reduce_sum(
                asum[:, j2:j2 + 1], b2[:, 0:512], axis=AX.X,
                apply_absolute_value=True), deps=[hl], inc=vc)

        h_terns = {}

        def emit_tern_piece(j):
            if j < 4:
                src_ap = w1s[:, j * 1024:(j + 1) * 1024]
                col = 0
            else:
                src_ap = w2s[(j - 4) * 128:(j - 3) * 128, :]
                col = 1
            buf = wst[j % 2]
            stg = wtt0
            hl = E("sync", lambda e, a=src_ap, b2=buf: e.dma_start(out=b2[:, :], in_=a),
                   deps=[h_terns.get(("cons", j - 2), st["nthr"]),
                         h_shr[15] if j <= 1 else None], inc=lw[j % 2], inc_n=16)
            hs1 = E("scalar", lambda e, b2=buf, c=col: e.activation(
                sgn1[:, :], b2[:, :], AF.Sign, bias=nthrb[:, c:c + 1]),
                deps=[hl, st["thrbc"], st["nthr"], h_terns.get(("comb", j - 1)),
                      h_terns.get(("gd", j - 2))], inc=sc)
            hs2 = E("scalar", lambda e, b2=buf, c=col: e.activation(
                sgn2[:, :], b2[:, :], AF.Sign, bias=thrb[:, c:c + 1]),
                deps=[hl, hs1], inc=sc)
            hc = E("gpsimd", lambda e, s2=stg: e.tensor_tensor(
                s2[:, :], sgn1[:, :], sgn2[:, :], op=ALU.add),
                deps=[hs2, h_terns.get(("gd", j - 1))], inc=gp)
            h_terns[("comb", j)] = hc
            h_terns[("cons", j)] = hs2
            h_terns[("gd", j)] = E("gpsimd", lambda e, j2=j, s2=stg: e.dma_start(
                out=gin3[j2, :, :], in_=s2[:, :]),
                deps=[hc, h_terns.get(("gd", j - 1))], inc=lwt, inc_n=16)

        def emit_w_readback(which, h_agw):
            hrb = None
            for cc in range(NC):
                if which == 0:
                    dst = w1t[:, cc * FF:(cc + 1) * FF].rearrange("p (s d) -> p s d", s=4)
                    srcap = gout3[8 * cc:8 * cc + 4, :, :].rearrange("s p d -> p s d")
                else:
                    dst = w2t[:, 4 * cc * DM:(4 * cc + 4) * DM].rearrange(
                        "p (s d) -> p s d", s=4)
                    srcap = gout3[8 * cc + 4:8 * cc + 8, :, :].rearrange("s p d -> p s d")
                hrb = E("scalar", lambda e, d2=dst, s2=srcap: e.dma_start(
                    out=d2, in_=s2),
                    deps=[h_agw] if cc == 0 else [], inc=(lwr1, lwr2)[which], inc_n=16)
            return hrb

        h_xn = [None] * NT
        h_xnTe = [None] * NT
        h_qds = [None] * NT
        h_ks = [None] * NT
        h_vs = [None] * NT
        h_vds = [None] * NT
        h_qdTe = [None] * NT
        h_kTe = [None] * NT
        h_P = [None] * NT
        h_rine = [None] * NT
        h_Mupd = [None] * NT
        h_MTbs = [None] * NT
        h_dMb = [None] * NT
        h_rmms = [None] * NT

        st = {"totc": None, "thr": None, "thrbc": None, "nthr": None,
              "wrb1": None, "wrb2": None}
        h_agw0 = [None]

        dCf = float(np.float64(DEC) ** TS)
        h_txp = [None]

        def emit_LN(i):
            p = i % 2
            if i >= 2:
                for bb in range(B):
                    h_xld[i] = E("sync", lambda e, i2=i, p2=p, b2=bb: e.dma_start(
                        out=xst[p2][b2 * TS:(b2 + 1) * TS, :],
                        in_=xc[b2, i2 * TS:(i2 + 1) * TS, :]),
                        deps=[h_xn[i - 2]] if bb == 0 else [],
                        inc=lx[p], inc_n=16)
            h_bn1 = E("vector", lambda e, p2=p: e.bn_stats(
                lnbn[:, 12 * p2:12 * p2 + 6], xst[p2][:, 0:512]),
                deps=[h_xld[i], h_xn[i - 2] if i >= 2 else None], inc=vc)
            h_bn2 = E("vector", lambda e, p2=p: e.bn_stats(
                lnbn[:, 12 * p2 + 6:12 * p2 + 12], xst[p2][:, 512:1024]),
                deps=[h_xld[i]], inc=vc)
            h_agr = E("vector", lambda e, p2=p: e.bn_aggr(
                lnagg[:, 2 * p2:2 * p2 + 2], lnbn[:, 12 * p2:12 * p2 + 12]),
                deps=[h_bn1, h_bn2], inc=vc)
            h_std = E("scalar", lambda e, p2=p: e.activation(
                lnstd[:, p2:p2 + 1], lnagg[:, 2 * p2 + 1:2 * p2 + 2], AF.Sqrt,
                bias=epsc[:, :]), deps=[h_agr, h_eps], inc=sc)
            h_rs = E("vector", lambda e, p2=p: e.reciprocal(
                lnrs[:, p2:p2 + 1], lnstd[:, p2:p2 + 1]),
                deps=[h_std, h_xn[i - 2] if i >= 2 else None], inc=vc)
            h_mb2s[i] = E("vector", lambda e, p2=p: e.tensor_scalar(
                lnmb[:, p2:p2 + 1], lnagg[:, 2 * p2:2 * p2 + 1], lnrs[:, p2:p2 + 1],
                -1.0, op0=ALU.mult, op1=ALU.mult), deps=[h_rs], inc=vc)

        def emit_TQ(i):
            p = i % 2
            h_xn[i] = E("scalar", lambda e, p2=p: e.activation(
                xnt_t[:, p2 * DM:(p2 + 1) * DM], xst[p2][:, :], AF.Identity,
                bias=lnmb[:, p2:p2 + 1], scale=lnrs[:, p2:p2 + 1]),
                deps=[h_mb2s[i], h_xn[i - 2] if i >= 2 else None,
                      h_txs[i - 2] if i >= 2 else None], inc=sc)
            for d in range(8):
                E("tensor", lambda e, p2=p, d2=d: e.transpose(
                    pT[p2][:, d2 * 128:(d2 + 1) * 128],
                    xnt_t[:, p2 * DM + d2 * 128:p2 * DM + (d2 + 1) * 128],
                    identb[:, :]),
                  deps=[h_xn[i], h_idc, h_qdTe[i - 2] if i >= 2 else None,
                        h_kTe[i - 2] if i >= 2 else None],
                  inc=(tn if d == 7 else None))
            h_tx = (tn, tn.count)
            h_txs[i] = h_tx
            h_xnTe[i] = E("vector", lambda e, p2=p, i2=i: e.tensor_copy(
                xnT[:, :].rearrange("q (d g) -> q d g", d=8)[:, :, i2 * 128:(i2 + 1) * 128],
                pT[p2][:, :].rearrange("q (d g) -> q d g", d=8)),
                deps=[h_tx], inc=vc)
            pQ = (pQK, pH0)[p]
            pV = (pVS, pH1)[p]
            ql = p * 256
            qk_prev = [h_qds[i - 2] if i >= 2 else None, h_ks[i - 2] if i >= 2 else None]
            for (dst_sl, w) in (((0, 256), cWq), ((256, 512), cWk)):
                for d in range(8):
                    E("tensor", lambda e, d2=d, sl2=dst_sl, w2=w, i2=i, pq=pQ: e.matmul(
                        pq[:, sl2[0]:sl2[1]],
                        xnT[:, d2 * 2048 + i2 * 128:d2 * 2048 + (i2 + 1) * 128],
                        w2[:, d2 * 256:(d2 + 1) * 256],
                        start=(d2 == 0), stop=(d2 == 7)),
                      deps=[h_xnTe[i], h_smallw] + qk_prev, inc=(tn if d == 7 else None))
            h_kmm = (tn, tn.count)
            for d in range(8):
                E("tensor", lambda e, d2=d, i2=i, pv=pV: e.matmul(
                    pv[:, 0:256],
                    xnT[:, d2 * 2048 + i2 * 128:d2 * 2048 + (i2 + 1) * 128],
                    cWv[:, d2 * 256:(d2 + 1) * 256],
                    start=(d2 == 0), stop=(d2 == 7)),
                  deps=[h_xnTe[i], h_smallw,
                        h_vs[i - 2] if i >= 2 else None, h_vds[i - 2] if i >= 2 else None,
                        h_P[i - 2] if i >= 2 else None],
                  inc=(tn if d == 7 else None))
            h_vmm = (tn, tn.count)
            h_qds[i] = E("scalar", lambda e, pq=pQ, q2=ql: e.activation(
                qd_s[:, q2:q2 + 256], pq[:, 0:256], AF.Identity, scale=uvec[:, :]),
                deps=[h_kmm, h_u], inc=sc)
            q4 = (i % 4) * 256
            h_ks[i] = E("scalar", lambda e, pq=pQ, q2=q4: e.copy(
                k_s[:, q2:q2 + 256], pq[:, 256:512]), deps=[h_kmm], inc=sc)
            h_vs[i] = E("scalar", lambda e, pv=pV, q2=q4: e.copy(
                v_s[:, q2:q2 + 256], pv[:, 0:256]), deps=[h_vmm], inc=sc)
            h_vds[i] = E("scalar", lambda e, pv=pV, q2=q4: e.activation(
                vd_s[:, q2:q2 + 256], pv[:, 0:256], AF.Identity, scale=dvvc[:, :]),
                deps=[h_vmm], inc=sc)
            for (g, src_, coff, qq2) in ((0, qd_s, 0, ql), (1, qd_s, 128, ql),
                                         (0, k_s, 256, q4), (1, k_s, 384, q4)):
                E("tensor", lambda e, p2=p, g2=g, s2=src_, c2=coff, q2=qq2: e.transpose(
                    pT[p2][:, 512 + c2:512 + c2 + 128],
                    s2[:, q2 + g2 * 128:q2 + (g2 + 1) * 128], identb[:, :]),
                  deps=[h_qds[i], h_ks[i], h_xnTe[i]],
                  inc=(tn if coff == 384 else None))
            h_t2 = (tn, tn.count)
            h_qdTe[i] = E("scalar", lambda e, p2=p, i2=i: e.copy(
                qdT[:, i2 * 256:(i2 + 1) * 256], pT[p2][:, 512:768]), deps=[h_t2], inc=sc)
            h_kTe[i] = E("scalar", lambda e, p2=p, q2=ql: e.copy(
                kT_s[:, q2:q2 + 256], pT[p2][:, 768:1024]), deps=[h_t2], inc=sc)

        def emit_AT1(i):
            p = i % 2
            pV = (pVS, pH1)[p]
            ql = p * 256
            pl = p * 128
            for g in range(2):
                E("tensor", lambda e, g2=g, i2=i, pv=pV, q2=ql: e.matmul(
                    pv[:, 256:384], kT_s[:, q2 + g2 * 128:q2 + (g2 + 1) * 128],
                    qdT[:, i2 * 256 + g2 * 128:i2 * 256 + (g2 + 1) * 128],
                    start=(g2 == 0), stop=(g2 == 1)),
                  deps=[h_qdTe[i], h_kTe[i], h_vs[i], h_vds[i]], inc=(tn if g == 1 else None))
            h_smm = (tn, tn.count)
            h_P[i] = E("vector", lambda e, pv=pV, p3=pl: e.tensor_tensor(
                ptb[:, p3:p3 + 128], pv[:, 256:384], dmaskc[:, :], op=ALU.mult),
                deps=[h_smm], inc=vc)

        def emit_AT2(i):
            p = i % 2
            ql = p * 256
            q4 = (i % 4) * 256
            pl = p * 128
            retr_deps = [h_P[i], h_vs[i], h_qdTe[i],
                         h_MTbs[i - 2] if i >= 2 else h_MTb,
                         h_dMb[i - 1] if i >= 1 else h_dMb0,
                         h_rine[i - 1] if i >= 1 else h_gbc]
            if i == 2:
                retr_deps.append(st["totc"])
            if i == 6:
                retr_deps.append(st["thrbc"])
            for h2 in range(2):
                E("tensor", lambda e, h3=h2, q2=q4, p3=pl: e.matmul(
                    pR[:, h3 * 128:(h3 + 1) * 128],
                    v_s[:, q2 + h3 * 128:q2 + (h3 + 1) * 128], ptb[:, p3:p3 + 128],
                    start=True, stop=False), deps=retr_deps, inc=None)
                for g in range(2):
                    E("tensor", lambda e, h3=h2, g2=g, i2=i, mo=(p * 512): e.matmul(
                        pR[:, h3 * 128:(h3 + 1) * 128],
                        MTbs[:, mo + g2 * 256 + h3 * 128:mo + g2 * 256 + (h3 + 1) * 128],
                        qdT[:, i2 * 256 + g2 * 128:i2 * 256 + (g2 + 1) * 128],
                        start=False, stop=False),
                      deps=retr_deps, inc=None)
                for g in range(2):
                    E("tensor", lambda e, h3=h2, g2=g, i2=i, mo=(((i - 1) % 2) * 512): e.matmul(
                        pR[:, h3 * 128:(h3 + 1) * 128],
                        dMb[:, mo + g2 * 256 + h3 * 128:mo + g2 * 256 + (h3 + 1) * 128],
                        qdT[:, i2 * 256 + g2 * 128:i2 * 256 + (g2 + 1) * 128],
                        start=False, stop=(g2 == 1)),
                      deps=retr_deps, inc=(tn if (h2 == 1 and g == 1) else None))
            h_rmm = (tn, tn.count)
            h_rmms[i] = h_rmm
            h_rine[i] = E("vector", lambda e, i2=i: e.tensor_copy(
                rin[:, i2 * 256:(i2 + 1) * 256], pR[:, 0:256]), deps=[h_rmm], inc=vc)
            for g in range(2):
                E("tensor", lambda e, g2=g, q2=q4: e.matmul(
                    pM[:, g2 * 256:(g2 + 1) * 256],
                    k_s[:, q2 + g2 * 128:q2 + (g2 + 1) * 128], vd_s[:, q2:q2 + 256],
                    start=True, stop=True),
                  deps=[h_ks[i], h_vds[i], h_Mupd[i - 1] if i >= 1 else None,
                        h_dMb[i - 1] if i >= 1 else None, h_rmm],
                  inc=(tn if g == 1 else None))
            h_dMmm = (tn, tn.count)
            h_dMb[i] = E("vector", lambda e, mo=(p * 512): e.tensor_copy(
                dMb[:, mo:mo + 512], pM[:, :]),
                deps=[h_dMmm, h_rmms[i - 1] if i >= 1 else None], inc=vc)
            h_mts = E("vector", lambda e: e.tensor_scalar(MT[:, :], MT[:, :], dCf, None,
                                                          op0=ALU.mult), deps=[h_dMmm], inc=vc)
            h_Mupd[i] = E("vector", lambda e: e.tensor_tensor(MT[:, :], MT[:, :], pM[:, :],
                                                              op=ALU.add), deps=[h_dMmm, h_mts], inc=vc)
            h_MTbs[i] = E("vector", lambda e, mo=(p * 512): e.tensor_scalar(
                MTbs[:, mo:mo + 512], MT[:, :], dCf, None, op0=ALU.mult),
                deps=[h_Mupd[i], h_rmm], inc=vc)

        h_txs = [None] * NT
        h_mb2s = [None] * NT
        for it in range(NT + 4):
            if it < 4:
                for j in range(4 * it, 4 * it + 4):
                    emit_shard_piece(j)
            if 2 <= it < NT + 2:
                emit_TQ(it - 2)
            if 3 <= it < NT + 3:
                emit_AT1(it - 3)
            if it < NT:
                emit_LN(it)
            if it < 4:
                continue
            i = it - 4
            emit_AT2(i)

            if i == 0:
                h_p2a[0] = E("vector", lambda e: e.reduce_sum(
                    part2[:, 0:1], asum[:, 0:8], axis=AX.X), deps=[h_shr[7]], inc=vc)
                h_p2b[0] = E("vector", lambda e: e.reduce_sum(
                    part2[:, 1:2], asum[:, 8:16], axis=AX.X), deps=[h_shr[15]], inc=vc)
            if i == 1:
                h_totmm = E("tensor", lambda e: e.matmul(
                    pR[0:1, 456:458], onescf[:, :], part2[:, :], start=True, stop=True),
                    deps=[h_p2a[0], h_p2b[0], h_consts, h_rine[1]], inc=tn)
                st["totc"] = E("vector", lambda e: e.tensor_copy(tot12[:, :], pR[0:1, 456:458]),
                               deps=[h_totmm], inc=vc)
                hg1 = E("gpsimd", lambda e: e.dma_start(out=gin2[:, :], in_=tot12[:, :]),
                        deps=[st["totc"]], inc=lgm, inc_n=16)
                h_ar = E("gpsimd", lambda e: e.collective_compute(
                    "AllReduce", ALU.add, replica_groups=[list(range(NC))],
                    ins=[gin2[:, :]], outs=[gout2[:, :]]), deps=[hg1], inc=gp)
                hg2 = E("gpsimd", lambda e: e.dma_start(out=tot12g[:, :], in_=gout2[:, :]),
                        deps=[h_ar], inc=lar, inc_n=16)
                st["thr"] = E("gpsimd", lambda e: e.tensor_scalar(
                    thr12[:, :], tot12g[:, :], 0.5 / NW, 0.5e-8, op0=ALU.mult, op1=ALU.max),
                    deps=[hg2], inc=gp)
            if i == 3:
                h_tz = E("gpsimd", lambda e: e.memset(ttmp[:, :], 0.0), inc=gp)
                h_tc = E("gpsimd", lambda e: e.tensor_copy(ttmp[0:1, :], thr12[:, :]),
                         deps=[st["thr"], h_tz], inc=gp)
            if i == 5:
                h_thrmm = E("tensor", lambda e: e.matmul(
                    pR[:, 458:460], ones128f[:, :], ttmp[:, :], start=True, stop=True),
                    deps=[h_tc, h_rine[5]], inc=tn)
                st["thrbc"] = E("scalar", lambda e: e.copy(thrb[:, :], pR[:, 458:460]),
                                deps=[h_thrmm], inc=sc)
                st["nthr"] = E("gpsimd", lambda e: e.tensor_scalar(
                    nthrb[:, :], thrb[:, :], -1.0, None, op0=ALU.mult),
                    deps=[st["thrbc"]], inc=gp)
            if i in (6, 7, 8, 9):
                emit_tern_piece(2 * (i - 6))
                emit_tern_piece(2 * (i - 6) + 1)
                if i == 9:
                    h_agw0[0] = E("gpsimd", lambda e: e.collective_compute(
                        "AllGather", ALU.bypass, replica_groups=[list(range(NC))],
                        ins=[gin3[:, :, :]], outs=[gout3[:, :, :]]),
                        deps=[h_terns[("gd", 7)]], inc=gp)

        hgm = []
        for g in range(2):
            hgm.append(E("gpsimd", lambda e, g2=g: e.dma_start(
                out=ginM[g2 * 128:(g2 + 1) * 128, :], in_=MT[:, g2 * 256:(g2 + 1) * 256]),
                deps=[h_Mupd[NT - 1]], inc=lgm, inc_n=16))
        h_ag = E("gpsimd", lambda e: e.collective_compute(
            "AllGather", ALU.bypass, replica_groups=[list(range(NC))],
            ins=[ginM[:, :]], outs=[goutM[:, :]]), deps=hgm, inc=gp)
        st["wrb1"] = emit_w_readback(0, h_agw0[0])
        st["wrb2"] = emit_w_readback(1, h_agw0[0])
        h_cmb = None
        for g in range(2):
            h_m0l = E("scalar", lambda e, g2=g: e.dma_start(
                out=agt[:, g2 * 256:(g2 + 1) * 256],
                in_=m0T[g2 * 128:(g2 + 1) * 128, :]),
                deps=[(lsh0, h_shr[15][1] if False else None)] if False else
                     ([h_shr[14], h_shr[15]] if g == 0 else []), inc=lag, inc_n=16)
        h_cmb = E("vector", lambda e: e.tensor_scalar(
            MsT[:, :], agt[:, :], pwc[:, 8:9], None, op0=ALU.mult),
            deps=[h_m0l, h_consts], inc=vc)
        for cp in range(NC):
            for g in range(2):
                hr = E("scalar", lambda e, c2=cp, g2=g: e.dma_start(
                    out=agt[:, g2 * 256:(g2 + 1) * 256],
                    in_=goutM[c2 * 256 + g2 * 128:c2 * 256 + (g2 + 1) * 128, :]),
                    deps=[h_ag, h_cmb] if g == 0 else [], inc=lag, inc_n=16)
            h_sc1 = E("vector", lambda e, c2=cp: e.tensor_scalar(
                yh[:, :], agt[:, :], pwc[:, c2:c2 + 1], None, op0=ALU.mult),
                deps=[hr], inc=vc)
            h_cmb = E("vector", lambda e: e.tensor_tensor(MsT[:, :], MsT[:, :], yh[:, :],
                                                          op=ALU.add), deps=[h_sc1], inc=vc)
        h_MsTb = E("vector", lambda e: e.tensor_copy(MsTb[:, :], MsT[:, :]), deps=[h_cmb], inc=vc)

        lacc0 = _Sem("lacc0", "dma", "gpsimd")
        lacc1 = _Sem("lacc1", "dma", "gpsimd")
        lacc0.handle = _st.enter_context(nc.semaphore("s_lacc0"))
        lacc1.handle = _st.enter_context(nc.semaphore("s_lacc1"))
        lacc = (lacc0, lacc1)
        yat = (yat0, yat1)
        h_aev = {}
        h_accD = {}
        h_pgprev = [h_rine[NT - 1]]

        def emit_attn_final(t):
            pg_deps = [h_MsTb, h_pgprev[0]]
            for h2 in range(2):
                for g in range(2):
                    E("tensor", lambda e, h3=h2, g2=g, t2=t: e.matmul(
                        pR[:, h3 * 128:(h3 + 1) * 128],
                        MsTb[:, g2 * 256 + h3 * 128:g2 * 256 + (h3 + 1) * 128],
                        qdT[:, t2 * 256 + g2 * 128:t2 * 256 + (g2 + 1) * 128],
                        start=(g2 == 0), stop=(g2 == 1)),
                      deps=pg_deps, inc=(tn if (h2 == 1 and g == 1) else None))
            h_pg = (tn, tn.count)
            h_r1 = E("vector", lambda e, t2=t: e.tensor_scalar(
                rtmp[:, :], pR[:, 0:256], float(np.float64(DEC) ** (TS * t2)), None,
                op0=ALU.mult), deps=[h_pg], inc=vc)
            h_pgprev[0] = h_r1
            h_rTs[t] = E("vector", lambda e, t2=t: e.tensor_tensor(
                rT_s[:, :], rtmp[:, :], rin[:, t2 * 256:(t2 + 1) * 256], op=ALU.add),
                deps=[h_r1, h_rTs[t - 1] if t >= 1 else None], inc=vc)
            for e2 in range(2):
                for h2 in range(2):
                    E("tensor", lambda e, e3=e2, h3=h2: e.matmul(
                        pM[:, 0:512], rT_s[:, h3 * 128:(h3 + 1) * 128],
                        cWo[:, h3 * 1024 + e3 * 512:h3 * 1024 + (e3 + 1) * 512],
                        start=(h3 == 0), stop=(h3 == 1)),
                      deps=[h_rTs[t], h_aev.get((t, 0)) if e2 == 1 else
                            (h_aev.get((t - 1, 1)) if t >= 1 else h_dMb[NT - 1]),
                            h_accD.get(t - 2)],
                      inc=(tn if h2 == 1 else None))
                h_pmm = (tn, tn.count)
                h_aev[(t, e2)] = E("vector", lambda e, t2=t, e3=e2: e.tensor_scalar(
                    yat[t2 % 2][:, e3 * 512:(e3 + 1) * 512], pM[:, 0:512], 0.5, None,
                    op0=ALU.mult), deps=[h_pmm, h_accD.get(t - 2)], inc=vc)
            hprev = h_accD.get(t - 2)
            for bb in range(B):
                hprev = E("gpsimd", lambda e, t2=t, b2=bb: e.dma_start(
                    out=yc[b2, t2 * TS:(t2 + 1) * TS, :],
                    in_=yat[t2 % 2][b2 * TS:(b2 + 1) * TS, :],
                    accum_op=ALU.add),
                    deps=[h_aev[(t, 0)], h_aev[(t, 1)], h_ydma[t], hprev]
                    if bb == 0 else [hprev], inc=lacc[t % 2], inc_n=16)
            h_accD[t] = hprev

        h_gelu = [None] * 32
        h_lastB = None
        h_yhev = {}
        h_yadd = {}
        h_ydma = [None] * NT
        h_xrr = [None] * NT
        h_rTs = [None] * NT
        prev_yh_user = h_MsTb
        for b in range(8):
            for f in range(32):
                a_deps = [st["wrb1"], h_gelu[(f - 2) % 32]]
                if b == 0 and f < 2:
                    a_deps += [h_qds[NT - 1], h_ks[NT - 1], h_vs[NT - 1],
                               h_vds[NT - 1], h_P[NT - 1]]
                for d in range(8):
                    E("tensor", lambda e, d2=d, f2=f, b2=b: e.matmul(
                        pH[f2 % 2][:, 0:256],
                        w1t[:, d2 * FF + f2 * 128:d2 * FF + (f2 + 1) * 128],
                        xnT[:, d2 * 2048 + b2 * 256:d2 * 2048 + (b2 + 1) * 256],
                        start=(d2 == 0), stop=(d2 == 7)),
                      deps=a_deps, inc=(tn if d == 7 else None))
                h_amm = (tn, tn.count)
                h_gelu[f] = E("scalar", lambda e, f2=f: e.activation(
                    hbuf[:, f2 * 256:(f2 + 1) * 256], pH[f2 % 2][:, 0:256], AF.Gelu,
                    scale=0.5, bias=b1rc[:, f2:f2 + 1]),
                    deps=[h_amm, h_consts, h_lastB], inc=sc)
            for tl in range(2):
                t = 2 * b + tl
                for bb in range(B):
                    h_xrr[t] = E("gpsimd", lambda e, t2=t, b2=bb: e.dma_start(
                        out=xst[t2 % 2][b2 * TS:(b2 + 1) * TS, :],
                        in_=xc[b2, t2 * TS:(t2 + 1) * TS, :]),
                        deps=([h_xn[NT - 1], h_yadd.get((t - 2, 1))]
                              if bb == 0 else []), inc=lrr[t % 2], inc_n=16)
                h_xrr[t] = E("gpsimd", lambda e, t2=t: e.tensor_tensor(
                    xst[t2 % 2][:, :], xst[t2 % 2][:, :], b2bc[:, :], op=ALU.add),
                    deps=[h_xrr[t], h_bev2[0]], inc=gp)
                grp_deps = [h_gelu[31], st["wrb2"],
                            h_yhev.get((t - 1, 0)), h_yhev.get((t - 1, 1)), h_smallw]
                if t == 0:
                    grp_deps += [h_qds[NT - 2], h_ks[NT - 2],
                                 h_vs[NT - 2], h_vds[NT - 2], h_P[NT - 2]]
                for f in range(32):
                    for e2 in range(2):
                        E("tensor", lambda e, e3=e2, f2=f, tl2=tl: e.matmul(
                            pO[e3][:, 0:512],
                            hbuf[:, f2 * 256 + tl2 * 128:f2 * 256 + (tl2 + 1) * 128],
                            w2t[:, f2 * DM + e3 * 512:f2 * DM + (e3 + 1) * 512],
                            start=(f2 == 0), stop=(f2 == 31)),
                          deps=grp_deps,
                          inc=(tn if (f == 31 and e2 == 1) else None))
                h_bmm = (tn, tn.count)
                if tl == 1:
                    h_lastB = h_bmm
                for e2 in range(2):
                    h_yhev[(t, e2)] = E("scalar", lambda e, e3=e2: e.activation(
                        yh[:, :], pO[e3][:, 0:512], AF.Identity, scale=0.5),
                        deps=[h_bmm, prev_yh_user], inc=sc)
                    h_yadd[(t, e2)] = E("vector", lambda e, t2=t, e3=e2: e.tensor_tensor(
                        ybuf[:, e3 * 512:(e3 + 1) * 512], yh[:, :],
                        xst[t2 % 2][:, e3 * 512:(e3 + 1) * 512], op=ALU.add),
                        deps=[h_yhev[(t, e2)], h_xrr[t],
                              h_ydma[t - 1] if t >= 1 else None], inc=vc)
                    prev_yh_user = h_yadd[(t, e2)]
                for bb in range(B):
                    h_ydma[t] = E("gpsimd", lambda e, t2=t, b2=bb: e.dma_start(
                        out=yc[b2, t2 * TS:(t2 + 1) * TS, :],
                        in_=ybuf[b2 * TS:(b2 + 1) * TS, :]),
                        deps=([h_yadd[(t, 0)], h_yadd[(t, 1)]] if bb == 0 else []),
                        inc=lyo, inc_n=16)
            sched = {2: (0, 1), 3: (2, 3, 4), 4: (5, 6, 7), 5: (8, 9, 10),
                     6: (11, 12, 13), 7: (14, 15)}
            for t in sched.get(b, ()):
                emit_attn_final(t)
        E("gpsimd", None, deps=[h_ydma[NT - 1], (lyo, lyo.count),
                                (lacc0, lacc0.count), (lacc1, lacc1.count)])

        @block.sync
        def _(eng):
            P.run_engine("sync", eng)

        @block.gpsimd
        def _(eng):
            P.run_engine("gpsimd", eng)

        @block.tensor
        def _(eng):
            P.run_engine("tensor", eng)

        @block.vector
        def _(eng):
            P.run_engine("vector", eng)

        @block.scalar
        def _(eng):
            P.run_engine("scalar", eng)

    return nc


def _consts(core):
    sloc = np.arange(128) % TS
    d64 = np.float64(DEC)
    dmask = np.zeros((128, 128), np.float64)
    for j in range(128):
        for t in range(128):
            if sloc[j] < sloc[t]:
                dmask[j, t] = d64 ** (-1.0 - sloc[j]) / B
    pw = np.zeros((128, 9), np.float64)
    for cp in range(core):
        pw[:, cp] = d64 ** (SEG * (core - 1 - cp))
    pw[:, 8] = d64 ** (SEG * core)
    return {
        "identf": np.eye(128, dtype=np.float32),
        "ones128": np.ones((128, 128), np.float32),
        "dmaskf": dmask.astype(np.float32),
        "dq2": (2.0 * d64 ** sloc).reshape(128, 1).astype(np.float32),
        "dvv": (d64 ** (TS - 1 - sloc) / B).reshape(128, 1).astype(np.float32),
        "pw": pw.astype(np.float32),
        "onescol": np.ones((128, 1), np.float32),
    }


_NC_CACHE = [None]


def kernel(L0_attn_x, L0_attn_Wq, L0_attn_Wk, L0_attn_Wv, L0_attn_Wo, L0_attn_gate,
           L0_attn_M0, L0_ffn_W1, L0_ffn_b1, L0_ffn_W2, L0_ffn_b2):
    x = np.ascontiguousarray(np.asarray(L0_attn_x, dtype=np.float32))
    wqT = np.ascontiguousarray(np.asarray(L0_attn_Wq, np.float32).T)
    wkT = np.ascontiguousarray(np.asarray(L0_attn_Wk, np.float32).T)
    wvT = np.ascontiguousarray(np.asarray(L0_attn_Wv, np.float32).T)
    woT = np.ascontiguousarray(np.asarray(L0_attn_Wo, np.float32).T)
    w1T = np.ascontiguousarray(np.asarray(L0_ffn_W1, np.float32).T)
    w2T = np.ascontiguousarray(np.asarray(L0_ffn_W2, np.float32).T)
    b1r = np.ascontiguousarray(np.asarray(L0_ffn_b1, np.float32).reshape(32, 128).T)
    b2r = np.ascontiguousarray(np.tile(np.asarray(L0_ffn_b2, np.float32).reshape(1, DM),
                                       (128, 1)))
    gate = np.asarray(L0_attn_gate, np.float32).reshape(1, 1)
    m0T = np.ascontiguousarray(np.asarray(L0_attn_M0, np.float32).T)

    if _NC_CACHE[0] is None:
        _NC_CACHE[0] = build_program()
    nc = _NC_CACHE[0]

    in_maps = []
    for c in range(NC):
        m = {
            "xc": np.ascontiguousarray(x[:, c * SEG:(c + 1) * SEG, :]),
            "wqT": wqT, "wkT": wkT, "wvT": wvT, "woT": woT,
            "w1s": np.ascontiguousarray(w1T[c * 128:(c + 1) * 128, :]),
            "w2s": np.ascontiguousarray(w2T[c * 512:(c + 1) * 512, :]),
            "b1r": b1r, "b2r": b2r, "gatei": gate, "m0T": m0T,
        }
        m.update(_consts(c))
        in_maps.append(m)
    res = run_bass_kernel_spmd(nc, in_maps, core_ids=list(range(NC)))
    out = np.concatenate([np.asarray(r["yc"]) for r in res.results], axis=1)
    return out
